# version 17
# baseline (speedup 1.0000x reference)
import numpy as np
import concourse.bass as bass
import concourse.mybir as mybir
from concourse.bass_utils import run_bass_kernel_spmd

F32 = mybir.dt.float32
BF16 = mybir.dt.bfloat16
I32 = mybir.dt.int32
AF = mybir.ActivationFunctionType
ALU = mybir.AluOpType

ENGS = ("pe", "act", "dve", "pool", "sp")


class Res:
    __slots__ = ("writer", "readers", "war0")

    def __init__(self, war0=None):
        self.writer = None
        self.readers = []
        self.war0 = list(war0 or [])


class Prog:

    def __init__(self):
        self.cnt = {}
        self.sched = {e: [] for e in ENGS}
        self.water = {e: {} for e in ENGS}
        self.sem_names = []
        for e in ENGS:
            self.sem("s_" + e)

    def sem(self, name):
        assert name not in self.cnt
        self.cnt[name] = 0
        self.sem_names.append(name)
        return name

    def op(self, eng, fn, reads=(), writes=(), sem=None, inc=1, extra=()):
        sem = sem or ("s_" + eng)
        deps = list(extra)
        for r in reads:
            if r.writer:
                deps.append(r.writer)
        for w in writes:
            if w.writer:
                deps.append(w.writer)
            else:
                deps.extend(w.war0)
            deps.extend(w.readers)
        need = {}
        for (s, v) in deps:
            if v > need.get(s, 0):
                need[s] = v
        wm = self.water[eng]
        waits = {}
        for s, v in sorted(need.items()):
            if eng == "pe" and s == "s_pe":
                continue
            if wm.get(s, 0) >= v:
                continue
            waits[s] = v
            wm[s] = v
        self.cnt[sem] += inc
        tok = (sem, self.cnt[sem])
        self.sched[eng].append((waits, fn, sem, inc))
        for r in reads:
            r.readers.append(tok)
        for w in writes:
            w.writer = tok
            w.readers = []
        return tok

    def fence(self, eng, deps):
        need = {}
        for (s, v) in deps:
            if v > need.get(s, 0):
                need[s] = v
        self.sched[eng].append((need, None, None, 0))

    def snapshot(self):
        return [(s, self.cnt[s]) for s in self.sem_names]


def build_nc(N=2048, D=1024, H=2816, NR=7):
    assert D == 1024
    NT = N // 128
    NI = D // 128
    NHC = H // 128
    NE = NR + 1
    HT = NT // 2
    EPS = 1e-5
    CTX_SCALE = (1.0 / 8.0) / 1024.0

    nc = bass.Bass("TRN2", target_bir_lowering=False)
    p = Prog()

    X = nc.dram_tensor("x", [N, D], F32, kind="ExternalInput").ap()
    OUT = nc.dram_tensor("out", [N, D], F32, kind="ExternalOutput").ap()
    WQ = nc.dram_tensor("wq", [D, D], F32, kind="ExternalInput").ap()
    WK = nc.dram_tensor("wk", [D, D], F32, kind="ExternalInput").ap()
    WV = nc.dram_tensor("wv", [D, D], F32, kind="ExternalInput").ap()
    WO = nc.dram_tensor("wo", [D, D], F32, kind="ExternalInput").ap()
    BO = nc.dram_tensor("bo", [D], F32, kind="ExternalInput").ap()
    WR = nc.dram_tensor("wr", [D, NR], F32, kind="ExternalInput").ap()
    RB = nc.dram_tensor("rb", [NR], F32, kind="ExternalInput").ap()
    SDN = nc.dram_tensor("sdn", [H, D], F32, kind="ExternalInput").ap()
    RD = nc.dram_tensor("rdn", [NR, H, D], F32, kind="ExternalInput").ap()
    WGU = nc.dram_tensor("wgu", [NE * NHC, 128, 2 * NI * 128], F32,
                         kind="ExternalInput").ap()

    Xt = X.rearrange("(t p) c -> t p c", p=128)
    OUTt = OUT.rearrange("(t p) c -> p t c", p=128)
    RDr = RD.rearrange("e (hc p) o -> e p hc o", p=128)
    dn_ap = [RDr[e] for e in range(NR)] + [SDN.rearrange("(hc p) o -> p hc o", p=128)]

    live = []

    def alloc(name, shape, dt, side="left"):
        g = nc.sbuf_tensor(name, shape, dt, side=side)
        h = g.__enter__()
        live.append((name, g))
        return h

    def free(name):
        for i in range(len(live) - 1, -1, -1):
            if live[i][0] == name:
                live[i][1].__exit__(None, None, None)
                del live[i]
                return
        raise KeyError(name)

    ones_bf = alloc("ones_bf", [128, 16], BF16)
    rb_bc = alloc("rb_bc", [128, NR], F32)
    wr_sb = alloc("wr_sb", [128, NI, NR], BF16)
    w_sb = alloc("w_sb", [128, NT * 8], F32)
    rsq = alloc("rsq", [128, NT], F32)
    rS = alloc("rS", [128, NI], F32)
    stat6 = alloc("stat6", [128, 12], F32)
    mv = alloc("mv", [128, 2], F32)
    veps = alloc("veps", [128, 1], F32)
    rinv = alloc("rinv", [128, 1], F32)
    rstd = alloc("rstd", [128, 1], F32)
    negm = alloc("negm", [128, 1], F32)
    nmr = alloc("nmr", [128, 1], F32)
    probs = alloc("probs", [128, 8], F32)
    m8 = alloc("m8", [128, 8], F32)
    msum = alloc("msum", [128, 1], F32)
    minv = alloc("minv", [128, 1], F32)
    maskb = alloc("maskb", [128, 8], F32)
    wtmp = alloc("wtmp", [128, 8], F32)
    qexpT = alloc("qexpT", [128, NI, N], BF16)
    xslot = [alloc("xslot0", [128, D], F32, side="right"),
             alloc("xslot1", [128, D], F32, side="right")]
    ident = alloc("ident", [128, 128], F32, side="right")
    iscr = alloc("iscr", [128, 128], I32, side="right")
    E_sb = alloc("E_sb", [128, NT, D], BF16, side="right")
    v_sb = alloc("v_sb", [128, NT, D], BF16, side="right")
    wk_sb = alloc("wk_sb", [128, NI, D], BF16, side="right")
    wv_sb = alloc("wv_sb", [128, NI, D], BF16, side="right")
    h1T = alloc("h1T", [128, NI, N], BF16, side="right")
    wq_sb = alloc("wq_sb", [128, NI, D], BF16, side="right")
    h1scr = alloc("h1scr", [128, 2 * D], F32, side="right")

    psg = nc.psum_tensor("PS", [128, 4096], F32)
    PSUM = psg.__enter__()

    PS = [Res() for _ in range(8)]
    ones_r = Res()
    ident_r = Res()
    iscr_r = Res()
    rb_r = Res()
    wr_r = Res()
    w_r = Res()
    rsq_r = Res()
    rS_r = Res()
    stat_r = Res()
    mv_r = Res()
    veps_r = Res()
    rinv_r = Res()
    rstd_r = Res()
    negm_r = Res()
    nmr_r = Res()
    probs_r = Res()
    m8_r = Res()
    msum_r = Res()
    minv_r = Res()
    mask_r = Res()
    wtmp_r = Res()
    qexpT_r = Res()
    ctx_r = Res()
    xslot_r = [Res(), Res()]
    E_r = Res()
    v_r = Res()
    wk_r = Res()
    wv_r = Res()
    h1T_r = [Res() for _ in range(NT // 4)]
    wq_r = Res()
    h1scr_r = [Res(), Res()]

    s_wq = p.sem("s_wq"); s_wk = p.sem("s_wk"); s_wv = p.sem("s_wv")
    s_wo = p.sem("s_wo"); s_wr = p.sem("s_wr"); s_bc = p.sem("s_bc")
    s_x = [p.sem("s_x0"), p.sem("s_x1")]
    s_ring = [p.sem("s_ring0"), p.sem("s_ring1")]
    s_wd = [p.sem("s_wd0"), p.sem("s_wd1")]; s_out = p.sem("s_out")

    p.op("pool", lambda g: g.dma_start(wq_sb[:], WQ.rearrange("(i q) o -> q i o", q=128)),
         writes=[wq_r], sem=s_wq, inc=16)
    p.op("pool", lambda g: g.dma_start(wk_sb[:], WK.rearrange("(i q) o -> q i o", q=128)),
         writes=[wk_r], sem=s_wk, inc=16)
    p.op("pool", lambda g: g.dma_start(wv_sb[:], WV.rearrange("(i q) o -> q i o", q=128)),
         writes=[wv_r], sem=s_wv, inc=16)
    p.op("pool", lambda g: g.dma_start(wr_sb[:], WR.rearrange("(i q) r -> q i r", q=128)),
         writes=[wr_r], sem=s_wr, inc=16)
    p.op("pool", lambda g: g.dma_start(rb_bc[:], RB.partition_broadcast(128)),
         writes=[rb_r], sem=s_bc, inc=16)
    p.op("pool", lambda g: g.iota(iscr[:], [[-1, 128]], base=0, channel_multiplier=1),
         writes=[iscr_r])
    p.op("pool", lambda g: g.tensor_scalar(ident[:], iscr[:], 0.0, None, ALU.is_equal),
         reads=[iscr_r], writes=[ident_r])
    p.op("act", lambda s: s.activation(rstd[:], rstd[:], AF.Exp, scale=0.0),
         writes=[rstd_r])
    p.op("dve", lambda v: v.memset(ones_bf[:], 1.0), writes=[ones_r])
    p.op("dve", lambda v: v.memset(probs[:, NR:8], -1.0), writes=[probs_r])

    def layernorm_chain(src_ap, src_res, scr, scr_res):
        p.op("dve", lambda v: v.bn_stats(stat6[:, 0:6], src_ap[:, 0:512]),
             reads=[src_res], writes=[stat_r])
        p.op("dve", lambda v: v.bn_stats(stat6[:, 6:12], src_ap[:, 512:1024]),
             reads=[src_res], writes=[stat_r])
        p.op("dve", lambda v: v.bn_aggr(mv[:], stat6[:]), reads=[stat_r], writes=[mv_r])
        p.op("dve", lambda v: v.tensor_scalar(veps[:], mv[:, 1:2], EPS, None, ALU.add),
             reads=[mv_r], writes=[veps_r])
        p.op("dve", lambda v: v.tensor_scalar(negm[:], mv[:, 0:1], -1.0, None, ALU.mult),
             reads=[mv_r], writes=[negm_r])
        p.op("act", lambda s: s.activation(rinv[:], veps[:], AF.Ln),
             reads=[veps_r], writes=[rinv_r])
        p.op("act", lambda s: s.activation(rstd[:], rinv[:], AF.Exp, scale=-0.5),
             reads=[rinv_r], writes=[rstd_r])
        p.op("dve", lambda v: v.tensor_tensor(nmr[:], negm[:], rstd[:], ALU.mult),
             reads=[negm_r, rstd_r], writes=[nmr_r])
        p.op("act", lambda s: s.activation(scr[:], src_ap, AF.Identity,
                                           scale=rstd[:], bias=nmr[:]),
             reads=[src_res, nmr_r, rstd_r], writes=[scr_res])

    def transpose_tile(t, scr, scr_res, dstT, dstT_res, banks):
        for half in range(2):
            bank = banks[half]

            def tfn(te, half=half, bank=bank):
                last = None
                for q in range(4):
                    ic = half * 4 + q
                    last = te.transpose(PSUM[:, bank * 512 + q * 128: bank * 512 + (q + 1) * 128],
                                        scr[:, ic * 128:(ic + 1) * 128], ident[:])
                return last
            p.op("pe", tfn, reads=[scr_res, ident_r], writes=[PS[bank]])

            def efn(e, half=half, bank=bank, t=t):
                dst = dstT[:, half * 4:(half + 1) * 4, t * 128:(t + 1) * 128]
                src = PSUM[:, bank * 512:(bank + 1) * 512]
                return e.activation(dst, src, AF.Copy)
            p.op("act", efn, reads=[PS[bank]], writes=[dstT_res])

    def layernorm_transpose(t, src_ap, src_res, scr, scr_res, dstT, dstT_res, banks):
        layernorm_chain(src_ap, src_res, scr, scr_res)
        transpose_tile(t, scr, scr_res, dstT, dstT_res, banks)

    for blk in range(NT // 4):
        ts = list(range(blk * 4, blk * 4 + 4))

        def ln_chain(t):
            sl = t % 2
            q = t % 2
            p.op("sp", lambda sp, t=t, sl=sl: sp.dma_start(xslot[sl][:], Xt[t]),
                 writes=[xslot_r[sl]], sem=s_x[sl], inc=16)
            layernorm_chain(xslot[sl][:, :], xslot_r[sl],
                            h1scr[:, q * D:(q + 1) * D], h1scr_r[q])

        def ln_T(t):
            q = t % 2
            transpose_tile(t, h1scr[:, q * D:(q + 1) * D], h1scr_r[q],
                           h1T, h1T_r[t // 4], (0, 1) if t % 2 == 0 else (2, 3))
        ln_chain(ts[0]); ln_chain(ts[1])
        ln_T(ts[0]); ln_chain(ts[2])
        ln_T(ts[1]); ln_chain(ts[3])
        ln_T(ts[2]); ln_T(ts[3])
        n5 = blk
        for oc in range(NI):
            bank = 4 + oc % 4

            def qfn(te, oc=oc, n5=n5, bank=bank):
                last = None
                for ic in range(NI):
                    last = te.matmul(PSUM[:, bank * 512:(bank + 1) * 512],
                                     wq_sb[:, ic, oc * 128:(oc + 1) * 128],
                                     h1T[:, ic, n5 * 512:(n5 + 1) * 512],
                                     start=(ic == 0), stop=(ic == NI - 1))
                return last
            p.op("pe", qfn, reads=[wq_r, h1T_r[n5]], writes=[PS[bank]])
            p.op("act", lambda s, oc=oc, n5=n5, bank=bank: s.activation(
                qexpT[:, oc, n5 * 512:(n5 + 1) * 512],
                PSUM[:, bank * 512:(bank + 1) * 512], AF.Exp),
                reads=[PS[bank]], writes=[qexpT_r])
    free("h1scr")
    free("wq_sb")

    for t in range(NT):
        bs = (0, 1, 2, 3) if t % 2 == 0 else (4, 5, 6, 7)

        def kvfn(te, t=t, bs=bs):
            last = None
            for ic in range(NI):
                st = ic == 0
                sp_ = ic == NI - 1
                lhs = h1T[:, ic, t * 128:(t + 1) * 128]
                last = te.matmul(PSUM[:, bs[0] * 512:(bs[0] + 1) * 512], lhs,
                                 wk_sb[:, ic, 0:512], start=st, stop=sp_)
                last = te.matmul(PSUM[:, bs[1] * 512:(bs[1] + 1) * 512], lhs,
                                 wk_sb[:, ic, 512:1024], start=st, stop=sp_)
                last = te.matmul(PSUM[:, bs[2] * 512:(bs[2] + 1) * 512], lhs,
                                 wv_sb[:, ic, 0:512], start=st, stop=sp_)
                last = te.matmul(PSUM[:, bs[3] * 512:(bs[3] + 1) * 512], lhs,
                                 wv_sb[:, ic, 512:1024], start=st, stop=sp_)
            return last
        p.op("pe", kvfn, reads=[h1T_r[t // 4], wk_r, wv_r],
             writes=[PS[bs[0]], PS[bs[1]], PS[bs[2]], PS[bs[3]]])
        p.op("act", lambda s, t=t, bs=bs: s.activation(
            E_sb[:, t, :], PSUM[:, bs[0] * 512:(bs[0] + 2) * 512], AF.Exp),
            reads=[PS[bs[0]], PS[bs[1]]], writes=[E_r])
        p.op("dve", lambda v, t=t, bs=bs: v.tensor_copy(
            v_sb[:, t, :], PSUM[:, bs[2] * 512:(bs[2] + 2) * 512]),
            reads=[PS[bs[2]], PS[bs[3]]], writes=[v_r])
    free("h1T"); free("wv_sb"); free("wk_sb")

    ctx = alloc("ctx", [128, NI, D], BF16)

    for dc in range(NI):
        bs = (0, 1, 2) if dc % 2 == 0 else (4, 5, 6)

        def cfn(te, dc=dc, bs=bs):
            last = None
            for t in range(NT):
                st = t == 0
                sp_ = t == NT - 1
                lhs = E_sb[:, t, dc * 128:(dc + 1) * 128]
                last = te.matmul(PSUM[:, bs[0] * 512:(bs[0] + 1) * 512], lhs,
                                 v_sb[:, t, 0:512], start=st, stop=sp_)
                last = te.matmul(PSUM[:, bs[1] * 512:(bs[1] + 1) * 512], lhs,
                                 v_sb[:, t, 512:1024], start=st, stop=sp_)
                last = te.matmul(PSUM[:, bs[2] * 512: bs[2] * 512 + 1], lhs,
                                 ones_bf[:, 0:1], start=st, stop=sp_)
            return last
        p.op("pe", cfn, reads=[E_r, v_r, ones_r], writes=[PS[bs[0]], PS[bs[1]], PS[bs[2]]])
        p.op("dve", lambda v, dc=dc, bs=bs: v.reciprocal(
            rS[:, dc:dc + 1], PSUM[:, bs[2] * 512: bs[2] * 512 + 1]),
            reads=[PS[bs[2]]], writes=[rS_r])
        p.op("dve", lambda v, dc=dc, bs=bs: v.tensor_scalar(
            ctx[:, dc, :], PSUM[:, bs[0] * 512:(bs[0] + 2) * 512],
            rS[:, dc:dc + 1], CTX_SCALE, ALU.mult, op1=ALU.mult),
            reads=[PS[bs[0]], PS[bs[1]], rS_r], writes=[ctx_r])
        for t in range(dc * NT // NI, (dc + 1) * NT // NI):
            bank = 3 if t % 2 == 0 else 7

            def sqfn(te, t=t, bank=bank):
                last = None
                for oc in range(NI):
                    last = te.matmul(PSUM[:, bank * 512: bank * 512 + 1],
                                     qexpT[:, oc, t * 128:(t + 1) * 128],
                                     ones_bf[:, 0:1], start=(oc == 0), stop=(oc == NI - 1))
                return last
            p.op("pe", sqfn, reads=[qexpT_r, ones_r], writes=[PS[bank]])
            p.op("dve", lambda v, t=t, bank=bank: v.reciprocal(
                rsq[:, t:t + 1], PSUM[:, bank * 512: bank * 512 + 1]),
                reads=[PS[bank]], writes=[rsq_r])
    free("v_sb"); free("E_sb"); free("iscr")
    snap_a5 = p.snapshot()

    yT = alloc("yT", [128, NI, N], BF16, side="right")
    wo_sb = alloc("wo_sb", [128, NI, D], BF16, side="right")
    bo_bc = alloc("bo_bc", [128, D], F32, side="right")
    yT_r = Res(war0=snap_a5)
    wo_r = Res(war0=snap_a5)
    bo_r = Res(war0=snap_a5)
    p.op("pool", lambda g: g.dma_start(wo_sb[:], WO.rearrange("(i q) o -> q i o", q=128)),
         writes=[wo_r], sem=s_wo, inc=16)
    p.op("pool", lambda g: g.dma_start(bo_bc[:], BO.partition_broadcast(128)),
         writes=[bo_r], sem=s_bc, inc=16)

    for ec in range(NI):
        for n5 in range(N // 512):
            bank = (ec * (N // 512) + n5) % 4

            def yfn(te, ec=ec, n5=n5, bank=bank):
                last = None
                for dc in range(NI):
                    last = te.matmul(PSUM[:, bank * 512:(bank + 1) * 512],
                                     ctx[:, dc, ec * 128:(ec + 1) * 128],
                                     qexpT[:, dc, n5 * 512:(n5 + 1) * 512],
                                     start=(dc == 0), stop=(dc == NI - 1))
                return last
            p.op("pe", yfn, reads=[ctx_r, qexpT_r], writes=[PS[bank]])
            p.op("act", lambda s, ec=ec, n5=n5, bank=bank: s.activation(
                yT[:, ec, n5 * 512:(n5 + 1) * 512],
                PSUM[:, bank * 512:(bank + 1) * 512], AF.Copy),
                reads=[PS[bank]], writes=[yT_r])
    free("ctx"); free("qexpT")
    snap_a7 = p.snapshot()

    x2 = alloc("x2", [128, NT * D], F32)
    h2T = alloc("h2T", [128, NI, N], BF16)
    x2_r = [Res(war0=snap_a7) for _ in range(NT)]
    h2T_r = [Res(war0=snap_a7) for _ in range(NT // 4)]
    xb = [alloc("xb0", [128, D], F32, side="right"),
          alloc("xb1", [128, D], F32, side="right")]
    xb_r = [Res(war0=snap_a7), Res(war0=snap_a7)]

    for t in range(NT):
        sl = t % 2
        p.op("sp", lambda sp, t=t, sl=sl: sp.dma_start(xslot[sl][:], Xt[t]),
             writes=[xslot_r[sl]], sem=s_x[sl], inc=16)
        b0 = 0 if t % 2 == 0 else 2
        b1 = b0 + 1

        def afn(te, t=t, b0=b0, b1=b1):
            last = None
            for ec in range(NI):
                st = ec == 0
                sp_ = ec == NI - 1
                lhs = yT[:, ec, t * 128:(t + 1) * 128]
                last = te.matmul(PSUM[:, b0 * 512:(b0 + 1) * 512], lhs,
                                 wo_sb[:, ec, 0:512], start=st, stop=sp_)
                last = te.matmul(PSUM[:, b1 * 512:(b1 + 1) * 512], lhs,
                                 wo_sb[:, ec, 512:1024], start=st, stop=sp_)
            return last
        p.op("pe", afn, reads=[yT_r, wo_r], writes=[PS[b0], PS[b1]])
        p.op("pool", lambda g, sl=sl: g.tensor_tensor(
            xb[sl][:], xslot[sl][:], bo_bc[:], ALU.add),
            reads=[xslot_r[sl], bo_r], writes=[xb_r[sl]])
        p.op("dve", lambda v, t=t, b0=b0, sl=sl: v.scalar_tensor_tensor(
            x2[:, t * D:(t + 1) * D], PSUM[:, b0 * 512:(b0 + 2) * 512],
            rsq[:, t:t + 1], xb[sl][:], ALU.mult, ALU.add),
            reads=[PS[b0], PS[b0 + 1], rsq_r, xb_r[sl]], writes=[x2_r[t]])
    free("xb1"); free("xb0"); free("bo_bc"); free("wo_sb"); free("yT")
    snap_a8 = p.snapshot()

    NHH = NHC // 2
    wd_sb = [alloc("wd_sb0", [128, NHH, D], BF16, side="right"),
             alloc("wd_sb1", [128, NHH, D], BF16, side="right")]
    hid = alloc("hid", [128, NHH, N], BF16, side="right")
    ring = [alloc("ring0", [128, 2, NI, 128], BF16, side="right"),
            alloc("ring1", [128, 2, NI, 128], BF16, side="right")]
    sg = [alloc("sg0", [128, 512], BF16, side="right"),
          alloc("sg1", [128, 512], BF16, side="right")]
    wd_r = [Res(war0=snap_a8), Res(war0=snap_a8)]
    hid_r = Res(war0=snap_a8)
    ring_r = [Res(war0=snap_a8), Res(war0=snap_a8)]
    sg_r = [Res(war0=snap_a8), Res(war0=snap_a8)]

    def emit_routing(t):
        bank = 4 if t % 2 == 0 else 5

        def lfn(te, t=t, bank=bank):
            last = None
            for ic in range(NI):
                last = te.matmul(PSUM[:, bank * 512: bank * 512 + NR],
                                 h2T[:, ic, t * 128:(t + 1) * 128],
                                 wr_sb[:, ic, :], start=(ic == 0), stop=(ic == NI - 1))
            return last
        p.op("pe", lfn, reads=[h2T_r[t // 4], wr_r], writes=[PS[bank]])
        p.op("dve", lambda v, bank=bank: v.tensor_tensor(
            wtmp[:, 0:NR], PSUM[:, bank * 512: bank * 512 + NR], rb_bc[:], ALU.mult),
            reads=[PS[bank], rb_r], writes=[wtmp_r])
        p.op("act", lambda s: s.activation(maskb[:, 0:NR], wtmp[:, 0:NR], AF.Exp,
                                           scale=-1.0),
             reads=[wtmp_r], writes=[mask_r])
        p.op("dve", lambda v: v.tensor_scalar(maskb[:, 0:NR], maskb[:, 0:NR], 1.0,
                                              None, ALU.add),
             reads=[mask_r], writes=[mask_r])
        p.op("dve", lambda v: v.reciprocal(probs[:, 0:NR], maskb[:, 0:NR]),
             reads=[mask_r], writes=[probs_r])
        p.op("dve", lambda v: v.max(m8[:], probs[:]), reads=[probs_r], writes=[m8_r])
        p.op("dve", lambda v: v.tensor_tensor(msum[:], m8[:, 0:1], m8[:, 1:2], ALU.add),
             reads=[m8_r], writes=[msum_r])
        p.op("dve", lambda v: v.reciprocal(minv[:], msum[:]), reads=[msum_r], writes=[minv_r])
        p.op("dve", lambda v: v.tensor_scalar(wtmp[:, 0:NR], probs[:, 0:NR], minv[:],
                                              None, ALU.mult),
             reads=[probs_r, minv_r], writes=[wtmp_r])
        p.op("dve", lambda v: v.tensor_scalar(maskb[:, 0:NR], probs[:, 0:NR], m8[:, 1:2],
                                              None, ALU.is_ge),
             reads=[probs_r, m8_r], writes=[mask_r])
        p.op("dve", lambda v, t=t: v.tensor_tensor(
            w_sb[:, t * 8: t * 8 + NR], wtmp[:, 0:NR], maskb[:, 0:NR], ALU.mult),
            reads=[wtmp_r, mask_r], writes=[w_r])


    h2scr = alloc("h2scr", [128, D], F32, side="right")
    h2scr_r = Res(war0=snap_a8)
    for blk in range(NT // 4):
        for t in range(blk * 4, blk * 4 + 4):
            layernorm_transpose(t, x2[:, t * D:(t + 1) * D], x2_r[t], h2scr, h2scr_r,
                                h2T, h2T_r[t // 4], banks=(0, 1) if t % 2 == 0 else (2, 3))
        for t in range(blk * 4, blk * 4 + 4):
            emit_routing(t)
    free("h2scr")

    N5 = N // 512
    for e in range(NE):
        for hh in range(2):
            hoff = hh * NHH
            wbuf = hh
            def emit_wd_chunk(ci, e=e, hh=hh, wbuf=wbuf, hoff=hoff):
                a, b_ = (0, NHH // 2) if ci == 0 else (NHH // 2, NHH)
                if a == b_:
                    return
                p.op("pool", lambda g: g.dma_start(
                    wd_sb[wbuf][:, a:b_, :], dn_ap[e][:, hoff + a:hoff + b_, :],
                    max_dma_last_dim=4096),
                    writes=[wd_r[wbuf]], sem=s_wd[wbuf], inc=16)
            wd_at = {}
            for ci in range(2):
                wd_at.setdefault(min(NHH - 1, 1 + ci * 4), []).append(ci)
            for hcl in range(NHH):
                hc = hoff + hcl
                slot = hcl % 2
                p.op("pool", lambda g, e=e, hc=hc, slot=slot: g.dma_start(
                    ring[slot][:], WGU[e * NHC + hc], max_dma_last_dim=4096),
                    writes=[ring_r[slot]], sem=s_ring[slot], inc=16)
                for ci in wd_at.get(hcl, ()):
                    emit_wd_chunk(ci)
                for n5 in range(N5):
                    par = (hcl * N5 + n5) % 2
                    gb = 0 if par == 0 else 2
                    ub = gb + 1
                    noff = n5 * 512

                    def gfn(te, slot=slot, gb=gb, noff=noff):
                        last = None
                        for ic in range(NI):
                            last = te.matmul(PSUM[:, gb * 512:(gb + 1) * 512],
                                             ring[slot][:, 0, ic, :],
                                             h2T[:, ic, noff:noff + 512],
                                             start=(ic == 0), stop=(ic == NI - 1))
                        return last

                    def ufn(te, slot=slot, ub=ub, noff=noff):
                        last = None
                        for ic in range(NI):
                            last = te.matmul(PSUM[:, ub * 512:(ub + 1) * 512],
                                             ring[slot][:, 1, ic, :],
                                             h2T[:, ic, noff:noff + 512],
                                             start=(ic == 0), stop=(ic == NI - 1))
                        return last
                    p.op("pe", gfn, reads=[ring_r[slot], h2T_r[n5]], writes=[PS[gb]])
                    p.op("pe", ufn, reads=[ring_r[slot], h2T_r[n5]], writes=[PS[ub]])
                    p.op("act", lambda s, gb=gb, par=par: s.activation(
                        sg[par][:], PSUM[:, gb * 512:(gb + 1) * 512], AF.Silu),
                        reads=[PS[gb]], writes=[sg_r[par]])
                    p.op("dve", lambda v, hcl=hcl, n5=n5, ub=ub, par=par: v.tensor_tensor(
                        hid[:, hcl, n5 * 512:(n5 + 1) * 512], sg[par][:],
                        PSUM[:, ub * 512:(ub + 1) * 512], ALU.mult),
                        reads=[sg_r[par], PS[ub]], writes=[hid_r])
            for sub in range(NT):
                t = sub
                b0 = 4 if sub % 2 == 0 else 6
                b1 = b0 + 1

                def dfn(te, sub=sub, b0=b0, b1=b1, wbuf=wbuf):
                    last = None
                    for hcl in range(NHH):
                        st = hcl == 0
                        sp_ = hcl == NHH - 1
                        lhs = hid[:, hcl, sub * 128:(sub + 1) * 128]
                        last = te.matmul(PSUM[:, b0 * 512:(b0 + 1) * 512], lhs,
                                         wd_sb[wbuf][:, hcl, 0:512], start=st, stop=sp_)
                        last = te.matmul(PSUM[:, b1 * 512:(b1 + 1) * 512], lhs,
                                         wd_sb[wbuf][:, hcl, 512:1024], start=st, stop=sp_)
                    return last
                p.op("pe", dfn, reads=[hid_r, wd_r[wbuf]], writes=[PS[b0], PS[b1]])
                wsc = w_sb[:, t * 8 + e: t * 8 + e + 1] if e < NR else 1.0
                rd = [PS[b0], PS[b1], x2_r[t]] + ([w_r] if e < NR else [])
                p.op("dve", lambda v, t=t, b0=b0, wsc=wsc: v.scalar_tensor_tensor(
                    x2[:, t * D:(t + 1) * D], PSUM[:, b0 * 512:(b0 + 2) * 512],
                    wsc, x2[:, t * D:(t + 1) * D], ALU.mult, ALU.add),
                    reads=rd, writes=[x2_r[t]])
                if e == NE - 1 and hh == 1 and sub % 2 == 1:
                    p.op("sp", lambda sp, t=t: sp.dma_start(
                        OUTt[:, t - 1:t + 1, :], x2[:, (t - 1) * D:(t + 1) * D]),
                        reads=[x2_r[t - 1], x2_r[t]], sem=s_out, inc=16)

    p.fence("sp", [(s_out, 16 * (NT // 2))])

    from contextlib import ExitStack
    with ExitStack() as es:
        semh = {name: es.enter_context(nc.semaphore(name)) for name in p.sem_names}
        block = es.enter_context(nc.Block())

        def run(engname, engobj):
            for waits, fn, sem, inc in p.sched[engname]:
                for sname, v in waits.items():
                    engobj.wait_ge(semh[sname], v)
                if fn is not None:
                    inst = fn(engobj)
                    inst.then_inc(semh[sem], inc)

        @block.tensor
        def _(t):
            run("pe", t)

        @block.scalar
        def _(s):
            run("act", s)

        @block.vector
        def _(v):
            run("dve", v)

        @block.gpsimd
        def _(g):
            run("pool", g)

        @block.sync
        def _(sp):
            run("sp", sp)

    for name, g in reversed(live):
        g.__exit__(None, None, None)
    psg.__exit__(None, None, None)
    return nc


def pack_gateup(gates, ups, D, H):
    NI = D // 128
    NHC = H // 128
    NE = len(gates)
    out = np.empty((NE * NHC, 128, 2 * NI * 128), np.float32)
    for e in range(NE):
        g = np.asarray(gates[e], np.float32).reshape(NI, 128, NHC, 128).transpose(2, 1, 0, 3)
        u = np.asarray(ups[e], np.float32).reshape(NI, 128, NHC, 128).transpose(2, 1, 0, 3)
        blk = np.stack([g, u], axis=2)
        out[e * NHC:(e + 1) * NHC] = blk.reshape(NHC, 128, 2 * NI * 128)
    return out


_NC_CACHE = {}


def _get_nc():
    if "nc" not in _NC_CACHE:
        _NC_CACHE["nc"] = build_nc()
    return _NC_CACHE["nc"]


def kernel(L0_attn_x, L0_attn_Wq, L0_attn_Wk, L0_attn_Wv, L0_attn_Wo, L0_attn_bo,
           L0_ffn_shared_gate, L0_ffn_shared_up, L0_ffn_shared_down,
           L0_ffn_routed_gate, L0_ffn_routed_up, L0_ffn_routed_down,
           L0_ffn_Wr, L0_ffn_routing_bias):
    c32 = lambda a: np.ascontiguousarray(np.asarray(a), dtype=np.float32)
    x = c32(L0_attn_x)
    rg = c32(L0_ffn_routed_gate)
    ru = c32(L0_ffn_routed_up)
    wgu = pack_gateup([rg[e] for e in range(rg.shape[0])] + [c32(L0_ffn_shared_gate)],
                      [ru[e] for e in range(ru.shape[0])] + [c32(L0_ffn_shared_up)],
                      D=1024, H=rg.shape[2])
    shared = {
        "wq": c32(L0_attn_Wq), "wk": c32(L0_attn_Wk), "wv": c32(L0_attn_Wv),
        "wo": c32(L0_attn_Wo), "bo": c32(L0_attn_bo), "wr": c32(L0_ffn_Wr),
        "rb": c32(L0_ffn_routing_bias),
        "sdn": c32(L0_ffn_shared_down),
        "rdn": c32(L0_ffn_routed_down),
        "wgu": wgu,
    }
    in_maps = [dict(shared, x=np.ascontiguousarray(x[c])) for c in range(8)]
    nc = _get_nc()
    res = run_bass_kernel_spmd(nc, in_maps, core_ids=list(range(8)))
    return np.stack([res.results[c]["out"] for c in range(8)], axis=0)


# version 19 (final)
# speedup vs baseline: 1.0475x; 1.0475x over previous
import numpy as np
import concourse.bass as bass
import concourse.mybir as mybir
from concourse.bass_utils import run_bass_kernel_spmd

F32 = mybir.dt.float32
BF16 = mybir.dt.bfloat16
I32 = mybir.dt.int32
AF = mybir.ActivationFunctionType
ALU = mybir.AluOpType

ENGS = ("pe", "act", "dve", "pool", "sp")


class Res:
    __slots__ = ("writer", "readers", "war0")

    def __init__(self, war0=None):
        self.writer = None
        self.readers = []
        self.war0 = list(war0 or [])


class Prog:

    def __init__(self):
        self.cnt = {}
        self.sched = {e: [] for e in ENGS}
        self.water = {e: {} for e in ENGS}
        self.sem_names = []
        for e in ENGS:
            self.sem("s_" + e)

    def sem(self, name):
        assert name not in self.cnt
        self.cnt[name] = 0
        self.sem_names.append(name)
        return name

    def op(self, eng, fn, reads=(), writes=(), sem=None, inc=1, extra=()):
        sem = sem or ("s_" + eng)
        deps = list(extra)
        for r in reads:
            if r.writer:
                deps.append(r.writer)
        for w in writes:
            if w.writer:
                deps.append(w.writer)
            else:
                deps.extend(w.war0)
            deps.extend(w.readers)
        need = {}
        for (s, v) in deps:
            if v > need.get(s, 0):
                need[s] = v
        wm = self.water[eng]
        waits = {}
        for s, v in sorted(need.items()):
            if eng == "pe" and s == "s_pe":
                continue
            if wm.get(s, 0) >= v:
                continue
            waits[s] = v
            wm[s] = v
        self.cnt[sem] += inc
        tok = (sem, self.cnt[sem])
        self.sched[eng].append((waits, fn, sem, inc))
        for r in reads:
            r.readers.append(tok)
        for w in writes:
            w.writer = tok
            w.readers = []
        return tok

    def fence(self, eng, deps):
        need = {}
        for (s, v) in deps:
            if v > need.get(s, 0):
                need[s] = v
        self.sched[eng].append((need, None, None, 0))

    def snapshot(self):
        return [(s, self.cnt[s]) for s in self.sem_names]


def build_nc(N=2048, D=1024, H=2816, NR=7):
    assert D == 1024
    NT = N // 128
    NI = D // 128
    NHC = H // 128
    NE = NR + 1
    HT = NT // 2
    EPS = 1e-5
    CTX_SCALE = (1.0 / 8.0) / 1024.0

    nc = bass.Bass("TRN2", target_bir_lowering=False)
    p = Prog()

    X = nc.dram_tensor("x", [N, D], F32, kind="ExternalInput").ap()
    OUT = nc.dram_tensor("out", [N, D], F32, kind="ExternalOutput").ap()
    WQ = nc.dram_tensor("wq", [D, D], F32, kind="ExternalInput").ap()
    WK = nc.dram_tensor("wk", [D, D], F32, kind="ExternalInput").ap()
    WV = nc.dram_tensor("wv", [D, D], F32, kind="ExternalInput").ap()
    WO = nc.dram_tensor("wo", [D, D], F32, kind="ExternalInput").ap()
    BO = nc.dram_tensor("bo", [D], F32, kind="ExternalInput").ap()
    WR = nc.dram_tensor("wr", [D, NR], F32, kind="ExternalInput").ap()
    RB = nc.dram_tensor("rb", [NR], F32, kind="ExternalInput").ap()
    SDN = nc.dram_tensor("sdn", [H, D], F32, kind="ExternalInput").ap()
    RD = nc.dram_tensor("rdn", [NR, H, D], F32, kind="ExternalInput").ap()
    WGU = nc.dram_tensor("wgu", [NE * NHC, 128, 2 * NI * 128], F32,
                         kind="ExternalInput").ap()

    Xt = X.rearrange("(t p) c -> t p c", p=128)
    OUTt = OUT.rearrange("(t p) c -> p t c", p=128)
    RDr = RD.rearrange("e (hc p) o -> e p hc o", p=128)
    dn_ap = [RDr[e] for e in range(NR)] + [SDN.rearrange("(hc p) o -> p hc o", p=128)]

    live = []

    def alloc(name, shape, dt, side="left"):
        g = nc.sbuf_tensor(name, shape, dt, side=side)
        h = g.__enter__()
        live.append((name, g))
        return h

    def free(name):
        for i in range(len(live) - 1, -1, -1):
            if live[i][0] == name:
                live[i][1].__exit__(None, None, None)
                del live[i]
                return
        raise KeyError(name)

    ones_bf = alloc("ones_bf", [128, 16], BF16)
    rb_bc = alloc("rb_bc", [128, NR], F32)
    wr_sb = alloc("wr_sb", [128, NI, NR], BF16)
    w_sb = alloc("w_sb", [128, NT * 8], F32)
    rsq = alloc("rsq", [128, NT], F32)
    rS = alloc("rS", [128, NI], F32)
    stat6 = alloc("stat6", [128, 12], F32)
    mv = alloc("mv", [128, 2], F32)
    veps = alloc("veps", [128, 1], F32)
    rinv = alloc("rinv", [128, 1], F32)
    rstd = alloc("rstd", [128, 1], F32)
    negm = alloc("negm", [128, 1], F32)
    nmr = alloc("nmr", [128, 1], F32)
    probs = alloc("probs", [128, 8], F32)
    m8 = alloc("m8", [128, 8], F32)
    msum = alloc("msum", [128, 1], F32)
    minv = alloc("minv", [128, 1], F32)
    maskb = alloc("maskb", [128, 8], F32)
    wtmp = alloc("wtmp", [128, 8], F32)
    qexpT = alloc("qexpT", [128, NI, N], BF16)
    xslot = [alloc("xslot0", [128, D], F32, side="right"),
             alloc("xslot1", [128, D], F32, side="right")]
    ident = alloc("ident", [128, 128], F32, side="right")
    iscr = alloc("iscr", [128, 128], I32, side="right")
    E_sb = alloc("E_sb", [128, NT, D], BF16, side="right")
    v_sb = alloc("v_sb", [128, NT, D], BF16, side="right")
    wk_sb = alloc("wk_sb", [128, NI, D], BF16, side="right")
    wv_sb = alloc("wv_sb", [128, NI, D], BF16, side="right")
    h1T = alloc("h1T", [128, NI, N], BF16, side="right")
    wq_sb = alloc("wq_sb", [128, NI, D], BF16, side="right")
    h1scr = alloc("h1scr", [128, 2 * D], F32, side="right")

    psg = nc.psum_tensor("PS", [128, 4096], F32)
    PSUM = psg.__enter__()

    PS = [Res() for _ in range(8)]
    ones_r = Res()
    ident_r = Res()
    iscr_r = Res()
    rb_r = Res()
    wr_r = Res()
    w_r = Res()
    rsq_r = Res()
    rS_r = Res()
    stat_r = Res()
    mv_r = Res()
    veps_r = Res()
    rinv_r = Res()
    rstd_r = Res()
    negm_r = Res()
    nmr_r = Res()
    probs_r = Res()
    m8_r = Res()
    msum_r = Res()
    minv_r = Res()
    mask_r = Res()
    wtmp_r = Res()
    qexpT_r = Res()
    ctx_r = Res()
    xslot_r = [Res(), Res()]
    E_r = Res()
    v_r = Res()
    wk_r = Res()
    wv_r = Res()
    h1T_r = [Res() for _ in range(NT // 4)]
    wq_r = Res()
    h1scr_r = [Res(), Res()]

    s_wq = p.sem("s_wq"); s_wk = p.sem("s_wk"); s_wv = p.sem("s_wv")
    s_wo = p.sem("s_wo"); s_wr = p.sem("s_wr"); s_bc = p.sem("s_bc")
    s_x = [p.sem("s_x0"), p.sem("s_x1")]
    s_ring = [p.sem("s_ring0"), p.sem("s_ring1")]
    s_wd = [p.sem("s_wd0"), p.sem("s_wd1")]; s_out = p.sem("s_out")

    p.op("pool", lambda g: g.dma_start(wq_sb[:], WQ.rearrange("(i q) o -> q i o", q=128)),
         writes=[wq_r], sem=s_wq, inc=16)
    p.op("pool", lambda g: g.dma_start(wk_sb[:], WK.rearrange("(i q) o -> q i o", q=128)),
         writes=[wk_r], sem=s_wk, inc=16)
    p.op("pool", lambda g: g.dma_start(wv_sb[:], WV.rearrange("(i q) o -> q i o", q=128)),
         writes=[wv_r], sem=s_wv, inc=16)
    p.op("pool", lambda g: g.dma_start(wr_sb[:], WR.rearrange("(i q) r -> q i r", q=128)),
         writes=[wr_r], sem=s_wr, inc=16)
    p.op("pool", lambda g: g.dma_start(rb_bc[:], RB.partition_broadcast(128)),
         writes=[rb_r], sem=s_bc, inc=16)
    p.op("pool", lambda g: g.iota(iscr[:], [[-1, 128]], base=0, channel_multiplier=1),
         writes=[iscr_r])
    p.op("pool", lambda g: g.tensor_scalar(ident[:], iscr[:], 0.0, None, ALU.is_equal),
         reads=[iscr_r], writes=[ident_r])
    p.op("act", lambda s: s.activation(rstd[:], rstd[:], AF.Exp, scale=0.0),
         writes=[rstd_r])
    p.op("dve", lambda v: v.memset(ones_bf[:], 1.0), writes=[ones_r])

    def warmfn(te):
        last = None
        for i in range(128):
            last = te.matmul(PSUM[0:16, 7 * 512: 7 * 512 + 16], ones_bf[:, 0:16],
                             ones_bf[:, 0:16], start=True, stop=True)
        return last
    p.op("pe", warmfn, reads=[ones_r], writes=[PS[7]])
    p.op("dve", lambda v: v.memset(probs[:, NR:8], -1.0), writes=[probs_r])

    def layernorm_chain(src_ap, src_res, scr, scr_res):
        p.op("dve", lambda v: v.bn_stats(stat6[:, 0:6], src_ap[:, 0:512]),
             reads=[src_res], writes=[stat_r])
        p.op("dve", lambda v: v.bn_stats(stat6[:, 6:12], src_ap[:, 512:1024]),
             reads=[src_res], writes=[stat_r])
        p.op("dve", lambda v: v.bn_aggr(mv[:], stat6[:]), reads=[stat_r], writes=[mv_r])
        p.op("dve", lambda v: v.tensor_scalar(veps[:], mv[:, 1:2], EPS, None, ALU.add),
             reads=[mv_r], writes=[veps_r])
        p.op("dve", lambda v: v.tensor_scalar(negm[:], mv[:, 0:1], -1.0, None, ALU.mult),
             reads=[mv_r], writes=[negm_r])
        p.op("act", lambda s: s.activation(rinv[:], veps[:], AF.Ln),
             reads=[veps_r], writes=[rinv_r])
        p.op("act", lambda s: s.activation(rstd[:], rinv[:], AF.Exp, scale=-0.5),
             reads=[rinv_r], writes=[rstd_r])
        p.op("dve", lambda v: v.tensor_tensor(nmr[:], negm[:], rstd[:], ALU.mult),
             reads=[negm_r, rstd_r], writes=[nmr_r])
        p.op("act", lambda s: s.activation(scr[:], src_ap, AF.Identity,
                                           scale=rstd[:], bias=nmr[:]),
             reads=[src_res, nmr_r, rstd_r], writes=[scr_res])

    def transpose_tile(t, scr, scr_res, dstT, dstT_res, banks):
        for half in range(2):
            bank = banks[half]

            def tfn(te, half=half, bank=bank):
                last = None
                for q in range(4):
                    ic = half * 4 + q
                    last = te.transpose(PSUM[:, bank * 512 + q * 128: bank * 512 + (q + 1) * 128],
                                        scr[:, ic * 128:(ic + 1) * 128], ident[:])
                return last
            p.op("pe", tfn, reads=[scr_res, ident_r], writes=[PS[bank]])

            def efn(e, half=half, bank=bank, t=t):
                dst = dstT[:, half * 4:(half + 1) * 4, t * 128:(t + 1) * 128]
                src = PSUM[:, bank * 512:(bank + 1) * 512]
                return e.activation(dst, src, AF.Copy)
            p.op("act", efn, reads=[PS[bank]], writes=[dstT_res])

    def layernorm_transpose(t, src_ap, src_res, scr, scr_res, dstT, dstT_res, banks):
        layernorm_chain(src_ap, src_res, scr, scr_res)
        transpose_tile(t, scr, scr_res, dstT, dstT_res, banks)

    for blk in range(NT // 4):
        ts = list(range(blk * 4, blk * 4 + 4))

        def ln_chain(t):
            sl = t % 2
            q = t % 2
            p.op("sp", lambda sp, t=t, sl=sl: sp.dma_start(xslot[sl][:], Xt[t]),
                 writes=[xslot_r[sl]], sem=s_x[sl], inc=16)
            layernorm_chain(xslot[sl][:, :], xslot_r[sl],
                            h1scr[:, q * D:(q + 1) * D], h1scr_r[q])

        def ln_T(t):
            q = t % 2
            transpose_tile(t, h1scr[:, q * D:(q + 1) * D], h1scr_r[q],
                           h1T, h1T_r[t // 4], (0, 1) if t % 2 == 0 else (2, 3))
        ln_chain(ts[0]); ln_chain(ts[1])
        ln_T(ts[0]); ln_chain(ts[2])
        ln_T(ts[1]); ln_chain(ts[3])
        ln_T(ts[2]); ln_T(ts[3])
        n5 = blk
        for oc in range(NI):
            bank = 4 + oc % 4

            def qfn(te, oc=oc, n5=n5, bank=bank):
                last = None
                for ic in range(NI):
                    last = te.matmul(PSUM[:, bank * 512:(bank + 1) * 512],
                                     wq_sb[:, ic, oc * 128:(oc + 1) * 128],
                                     h1T[:, ic, n5 * 512:(n5 + 1) * 512],
                                     start=(ic == 0), stop=(ic == NI - 1))
                return last
            p.op("pe", qfn, reads=[wq_r, h1T_r[n5]], writes=[PS[bank]])
            p.op("act", lambda s, oc=oc, n5=n5, bank=bank: s.activation(
                qexpT[:, oc, n5 * 512:(n5 + 1) * 512],
                PSUM[:, bank * 512:(bank + 1) * 512], AF.Exp),
                reads=[PS[bank]], writes=[qexpT_r])
    free("h1scr")
    free("wq_sb")

    for t in range(NT):
        bs = (0, 1, 2, 3) if t % 2 == 0 else (4, 5, 6, 7)

        def kvfn(te, t=t, bs=bs):
            last = None
            for ic in range(NI):
                st = ic == 0
                sp_ = ic == NI - 1
                lhs = h1T[:, ic, t * 128:(t + 1) * 128]
                last = te.matmul(PSUM[:, bs[0] * 512:(bs[0] + 1) * 512], lhs,
                                 wk_sb[:, ic, 0:512], start=st, stop=sp_)
                last = te.matmul(PSUM[:, bs[1] * 512:(bs[1] + 1) * 512], lhs,
                                 wk_sb[:, ic, 512:1024], start=st, stop=sp_)
                last = te.matmul(PSUM[:, bs[2] * 512:(bs[2] + 1) * 512], lhs,
                                 wv_sb[:, ic, 0:512], start=st, stop=sp_)
                last = te.matmul(PSUM[:, bs[3] * 512:(bs[3] + 1) * 512], lhs,
                                 wv_sb[:, ic, 512:1024], start=st, stop=sp_)
            return last
        p.op("pe", kvfn, reads=[h1T_r[t // 4], wk_r, wv_r],
             writes=[PS[bs[0]], PS[bs[1]], PS[bs[2]], PS[bs[3]]])
        p.op("act", lambda s, t=t, bs=bs: s.activation(
            E_sb[:, t, :], PSUM[:, bs[0] * 512:(bs[0] + 2) * 512], AF.Exp),
            reads=[PS[bs[0]], PS[bs[1]]], writes=[E_r])
        p.op("dve", lambda v, t=t, bs=bs: v.tensor_copy(
            v_sb[:, t, :], PSUM[:, bs[2] * 512:(bs[2] + 2) * 512]),
            reads=[PS[bs[2]], PS[bs[3]]], writes=[v_r])
    free("h1T"); free("wv_sb"); free("wk_sb")

    ctx = alloc("ctx", [128, NI, D], BF16)

    for dc in range(NI):
        bs = (0, 1, 2) if dc % 2 == 0 else (4, 5, 6)

        def cfn(te, dc=dc, bs=bs):
            last = None
            for t in range(NT):
                st = t == 0
                sp_ = t == NT - 1
                lhs = E_sb[:, t, dc * 128:(dc + 1) * 128]
                last = te.matmul(PSUM[:, bs[0] * 512:(bs[0] + 1) * 512], lhs,
                                 v_sb[:, t, 0:512], start=st, stop=sp_)
                last = te.matmul(PSUM[:, bs[1] * 512:(bs[1] + 1) * 512], lhs,
                                 v_sb[:, t, 512:1024], start=st, stop=sp_)
                last = te.matmul(PSUM[:, bs[2] * 512: bs[2] * 512 + 1], lhs,
                                 ones_bf[:, 0:1], start=st, stop=sp_)
            return last
        p.op("pe", cfn, reads=[E_r, v_r, ones_r], writes=[PS[bs[0]], PS[bs[1]], PS[bs[2]]])
        p.op("dve", lambda v, dc=dc, bs=bs: v.reciprocal(
            rS[:, dc:dc + 1], PSUM[:, bs[2] * 512: bs[2] * 512 + 1]),
            reads=[PS[bs[2]]], writes=[rS_r])
        p.op("dve", lambda v, dc=dc, bs=bs: v.tensor_scalar(
            ctx[:, dc, :], PSUM[:, bs[0] * 512:(bs[0] + 2) * 512],
            rS[:, dc:dc + 1], CTX_SCALE, ALU.mult, op1=ALU.mult),
            reads=[PS[bs[0]], PS[bs[1]], rS_r], writes=[ctx_r])
        for t in range(dc * NT // NI, (dc + 1) * NT // NI):
            bank = 3 if t % 2 == 0 else 7

            def sqfn(te, t=t, bank=bank):
                last = None
                for oc in range(NI):
                    last = te.matmul(PSUM[:, bank * 512: bank * 512 + 1],
                                     qexpT[:, oc, t * 128:(t + 1) * 128],
                                     ones_bf[:, 0:1], start=(oc == 0), stop=(oc == NI - 1))
                return last
            p.op("pe", sqfn, reads=[qexpT_r, ones_r], writes=[PS[bank]])
            p.op("dve", lambda v, t=t, bank=bank: v.reciprocal(
                rsq[:, t:t + 1], PSUM[:, bank * 512: bank * 512 + 1]),
                reads=[PS[bank]], writes=[rsq_r])
    free("v_sb"); free("E_sb"); free("iscr")
    snap_a5 = p.snapshot()

    yT = alloc("yT", [128, NI, N], BF16, side="right")
    wo_sb = alloc("wo_sb", [128, NI, D], BF16, side="right")
    bo_bc = alloc("bo_bc", [128, D], F32, side="right")
    yT_r = Res(war0=snap_a5)
    wo_r = Res(war0=snap_a5)
    bo_r = Res(war0=snap_a5)
    p.op("pool", lambda g: g.dma_start(wo_sb[:], WO.rearrange("(i q) o -> q i o", q=128)),
         writes=[wo_r], sem=s_wo, inc=16)
    p.op("pool", lambda g: g.dma_start(bo_bc[:], BO.partition_broadcast(128)),
         writes=[bo_r], sem=s_bc, inc=16)

    for ec in range(NI):
        for n5 in range(N // 512):
            bank = (ec * (N // 512) + n5) % 4

            def yfn(te, ec=ec, n5=n5, bank=bank):
                last = None
                for dc in range(NI):
                    last = te.matmul(PSUM[:, bank * 512:(bank + 1) * 512],
                                     ctx[:, dc, ec * 128:(ec + 1) * 128],
                                     qexpT[:, dc, n5 * 512:(n5 + 1) * 512],
                                     start=(dc == 0), stop=(dc == NI - 1))
                return last
            p.op("pe", yfn, reads=[ctx_r, qexpT_r], writes=[PS[bank]])
            p.op("act", lambda s, ec=ec, n5=n5, bank=bank: s.activation(
                yT[:, ec, n5 * 512:(n5 + 1) * 512],
                PSUM[:, bank * 512:(bank + 1) * 512], AF.Copy),
                reads=[PS[bank]], writes=[yT_r])
    free("ctx"); free("qexpT")
    snap_a7 = p.snapshot()

    x2 = alloc("x2", [128, NT * D], F32)
    h2T = alloc("h2T", [128, NI, N], BF16)
    x2_r = [Res(war0=snap_a7) for _ in range(NT)]
    h2T_r = [Res(war0=snap_a7) for _ in range(NT // 4)]
    xb = [alloc("xb0", [128, D], F32, side="right"),
          alloc("xb1", [128, D], F32, side="right")]
    xb_r = [Res(war0=snap_a7), Res(war0=snap_a7)]

    for t in range(NT):
        sl = t % 2
        p.op("sp", lambda sp, t=t, sl=sl: sp.dma_start(xslot[sl][:], Xt[t]),
             writes=[xslot_r[sl]], sem=s_x[sl], inc=16)
        b0 = 0 if t % 2 == 0 else 2
        b1 = b0 + 1

        def afn(te, t=t, b0=b0, b1=b1):
            last = None
            for ec in range(NI):
                st = ec == 0
                sp_ = ec == NI - 1
                lhs = yT[:, ec, t * 128:(t + 1) * 128]
                last = te.matmul(PSUM[:, b0 * 512:(b0 + 1) * 512], lhs,
                                 wo_sb[:, ec, 0:512], start=st, stop=sp_)
                last = te.matmul(PSUM[:, b1 * 512:(b1 + 1) * 512], lhs,
                                 wo_sb[:, ec, 512:1024], start=st, stop=sp_)
            return last
        p.op("pe", afn, reads=[yT_r, wo_r], writes=[PS[b0], PS[b1]])
        p.op("pool", lambda g, sl=sl: g.tensor_tensor(
            xb[sl][:], xslot[sl][:], bo_bc[:], ALU.add),
            reads=[xslot_r[sl], bo_r], writes=[xb_r[sl]])
        p.op("dve", lambda v, t=t, b0=b0, sl=sl: v.scalar_tensor_tensor(
            x2[:, t * D:(t + 1) * D], PSUM[:, b0 * 512:(b0 + 2) * 512],
            rsq[:, t:t + 1], xb[sl][:], ALU.mult, ALU.add),
            reads=[PS[b0], PS[b0 + 1], rsq_r, xb_r[sl]], writes=[x2_r[t]])
    free("xb1"); free("xb0"); free("bo_bc"); free("wo_sb"); free("yT")
    snap_a8 = p.snapshot()

    NHH = NHC // 2
    wd_sb = [alloc("wd_sb0", [128, NHH, D], BF16, side="right"),
             alloc("wd_sb1", [128, NHH, D], BF16, side="right")]
    ring = [alloc("ring0", [128, 2, NI, 128], BF16, side="right"),
            alloc("ring1", [128, 2, NI, 128], BF16, side="right")]
    sg = [alloc("sg0", [128, 512], BF16, side="right"),
          alloc("sg1", [128, 512], BF16, side="right")]
    wd_r = [Res(war0=snap_a8), Res(war0=snap_a8)]
    ring_r = [Res(war0=snap_a8), Res(war0=snap_a8)]
    sg_r = [Res(war0=snap_a8), Res(war0=snap_a8)]

    def emit_routing(t):
        bank = 4 if t % 2 == 0 else 5

        def lfn(te, t=t, bank=bank):
            last = None
            for ic in range(NI):
                last = te.matmul(PSUM[:, bank * 512: bank * 512 + NR],
                                 h2T[:, ic, t * 128:(t + 1) * 128],
                                 wr_sb[:, ic, :], start=(ic == 0), stop=(ic == NI - 1))
            return last
        p.op("pe", lfn, reads=[h2T_r[t // 4], wr_r], writes=[PS[bank]])
        p.op("dve", lambda v, bank=bank: v.tensor_tensor(
            wtmp[:, 0:NR], PSUM[:, bank * 512: bank * 512 + NR], rb_bc[:], ALU.mult),
            reads=[PS[bank], rb_r], writes=[wtmp_r])
        p.op("act", lambda s: s.activation(maskb[:, 0:NR], wtmp[:, 0:NR], AF.Exp,
                                           scale=-1.0),
             reads=[wtmp_r], writes=[mask_r])
        p.op("dve", lambda v: v.tensor_scalar(maskb[:, 0:NR], maskb[:, 0:NR], 1.0,
                                              None, ALU.add),
             reads=[mask_r], writes=[mask_r])
        p.op("dve", lambda v: v.reciprocal(probs[:, 0:NR], maskb[:, 0:NR]),
             reads=[mask_r], writes=[probs_r])
        p.op("dve", lambda v: v.max(m8[:], probs[:]), reads=[probs_r], writes=[m8_r])
        p.op("dve", lambda v: v.tensor_tensor(msum[:], m8[:, 0:1], m8[:, 1:2], ALU.add),
             reads=[m8_r], writes=[msum_r])
        p.op("dve", lambda v: v.reciprocal(minv[:], msum[:]), reads=[msum_r], writes=[minv_r])
        p.op("dve", lambda v: v.tensor_scalar(wtmp[:, 0:NR], probs[:, 0:NR], minv[:],
                                              None, ALU.mult),
             reads=[probs_r, minv_r], writes=[wtmp_r])
        p.op("dve", lambda v: v.tensor_scalar(maskb[:, 0:NR], probs[:, 0:NR], m8[:, 1:2],
                                              None, ALU.is_ge),
             reads=[probs_r, m8_r], writes=[mask_r])
        p.op("dve", lambda v, t=t: v.tensor_tensor(
            w_sb[:, t * 8: t * 8 + NR], wtmp[:, 0:NR], maskb[:, 0:NR], ALU.mult),
            reads=[wtmp_r, mask_r], writes=[w_r])


    h2scr = alloc("h2scr", [128, 2 * D], F32, side="right")
    h2scr_r = [Res(war0=snap_a8), Res(war0=snap_a8)]
    for blk in range(NT // 4):
        ts = list(range(blk * 4, blk * 4 + 4))

        def ln2_chain(t):
            q = t % 2
            layernorm_chain(x2[:, t * D:(t + 1) * D], x2_r[t],
                            h2scr[:, q * D:(q + 1) * D], h2scr_r[q])

        def ln2_T(t):
            q = t % 2
            transpose_tile(t, h2scr[:, q * D:(q + 1) * D], h2scr_r[q],
                           h2T, h2T_r[t // 4], (0, 1) if t % 2 == 0 else (2, 3))
        ln2_chain(ts[0]); ln2_chain(ts[1])
        ln2_T(ts[0]); ln2_chain(ts[2])
        ln2_T(ts[1]); ln2_chain(ts[3])
        ln2_T(ts[2]); ln2_T(ts[3])
        for t in ts:
            emit_routing(t)
    free("h2scr")
    hid = alloc("hid", [128, NHH, N], BF16, side="right")
    hid_r = Res(war0=p.snapshot())

    N5 = N // 512
    for e in range(NE):
        for hh in range(2):
            hoff = hh * NHH
            wbuf = hh
            def emit_wd_chunk(ci, e=e, hh=hh, wbuf=wbuf, hoff=hoff):
                a, b_ = (0, NHH // 2) if ci == 0 else (NHH // 2, NHH)
                if a == b_:
                    return
                p.op("pool", lambda g: g.dma_start(
                    wd_sb[wbuf][:, a:b_, :], dn_ap[e][:, hoff + a:hoff + b_, :],
                    max_dma_last_dim=4096),
                    writes=[wd_r[wbuf]], sem=s_wd[wbuf], inc=16)
            wd_at = {}
            for ci in range(2):
                wd_at.setdefault(min(NHH - 1, 1 + ci * 4), []).append(ci)
            for hcl in range(NHH):
                hc = hoff + hcl
                slot = hcl % 2
                p.op("pool", lambda g, e=e, hc=hc, slot=slot: g.dma_start(
                    ring[slot][:], WGU[e * NHC + hc], max_dma_last_dim=4096),
                    writes=[ring_r[slot]], sem=s_ring[slot], inc=16)
                for ci in wd_at.get(hcl, ()):
                    emit_wd_chunk(ci)
                for n5 in range(N5):
                    par = (hcl * N5 + n5) % 2
                    gb = 0 if par == 0 else 2
                    ub = gb + 1
                    noff = n5 * 512

                    def gfn(te, slot=slot, gb=gb, noff=noff):
                        last = None
                        for ic in range(NI):
                            last = te.matmul(PSUM[:, gb * 512:(gb + 1) * 512],
                                             ring[slot][:, 0, ic, :],
                                             h2T[:, ic, noff:noff + 512],
                                             start=(ic == 0), stop=(ic == NI - 1))
                        return last

                    def ufn(te, slot=slot, ub=ub, noff=noff):
                        last = None
                        for ic in range(NI):
                            last = te.matmul(PSUM[:, ub * 512:(ub + 1) * 512],
                                             ring[slot][:, 1, ic, :],
                                             h2T[:, ic, noff:noff + 512],
                                             start=(ic == 0), stop=(ic == NI - 1))
                        return last
                    p.op("pe", gfn, reads=[ring_r[slot], h2T_r[n5]], writes=[PS[gb]])
                    p.op("pe", ufn, reads=[ring_r[slot], h2T_r[n5]], writes=[PS[ub]])
                    p.op("act", lambda s, gb=gb, par=par: s.activation(
                        sg[par][:], PSUM[:, gb * 512:(gb + 1) * 512], AF.Silu),
                        reads=[PS[gb]], writes=[sg_r[par]])
                    p.op("dve", lambda v, hcl=hcl, n5=n5, ub=ub, par=par: v.tensor_tensor(
                        hid[:, hcl, n5 * 512:(n5 + 1) * 512], sg[par][:],
                        PSUM[:, ub * 512:(ub + 1) * 512], ALU.mult),
                        reads=[sg_r[par], PS[ub]], writes=[hid_r])
            for sub in range(NT):
                t = sub
                b0 = 4 if sub % 2 == 0 else 6
                b1 = b0 + 1

                def dfn(te, sub=sub, b0=b0, b1=b1, wbuf=wbuf):
                    last = None
                    for hcl in range(NHH):
                        st = hcl == 0
                        sp_ = hcl == NHH - 1
                        lhs = hid[:, hcl, sub * 128:(sub + 1) * 128]
                        last = te.matmul(PSUM[:, b0 * 512:(b0 + 1) * 512], lhs,
                                         wd_sb[wbuf][:, hcl, 0:512], start=st, stop=sp_)
                        last = te.matmul(PSUM[:, b1 * 512:(b1 + 1) * 512], lhs,
                                         wd_sb[wbuf][:, hcl, 512:1024], start=st, stop=sp_)
                    return last
                p.op("pe", dfn, reads=[hid_r, wd_r[wbuf]], writes=[PS[b0], PS[b1]])
                wsc = w_sb[:, t * 8 + e: t * 8 + e + 1] if e < NR else 1.0
                rd = [PS[b0], PS[b1], x2_r[t]] + ([w_r] if e < NR else [])
                p.op("dve", lambda v, t=t, b0=b0, wsc=wsc: v.scalar_tensor_tensor(
                    x2[:, t * D:(t + 1) * D], PSUM[:, b0 * 512:(b0 + 2) * 512],
                    wsc, x2[:, t * D:(t + 1) * D], ALU.mult, ALU.add),
                    reads=rd, writes=[x2_r[t]])
                if e == NE - 1 and hh == 1 and sub % 2 == 1:
                    p.op("sp", lambda sp, t=t: sp.dma_start(
                        OUTt[:, t - 1:t + 1, :], x2[:, (t - 1) * D:(t + 1) * D]),
                        reads=[x2_r[t - 1], x2_r[t]], sem=s_out, inc=16)

    p.fence("sp", [(s_out, 16 * (NT // 2))])

    from contextlib import ExitStack
    with ExitStack() as es:
        semh = {name: es.enter_context(nc.semaphore(name)) for name in p.sem_names}
        block = es.enter_context(nc.Block())

        def run(engname, engobj):
            for waits, fn, sem, inc in p.sched[engname]:
                for sname, v in waits.items():
                    engobj.wait_ge(semh[sname], v)
                if fn is not None:
                    inst = fn(engobj)
                    inst.then_inc(semh[sem], inc)

        @block.tensor
        def _(t):
            run("pe", t)

        @block.scalar
        def _(s):
            run("act", s)

        @block.vector
        def _(v):
            run("dve", v)

        @block.gpsimd
        def _(g):
            run("pool", g)

        @block.sync
        def _(sp):
            run("sp", sp)

    for name, g in reversed(live):
        g.__exit__(None, None, None)
    psg.__exit__(None, None, None)
    return nc


def pack_gateup(gates, ups, D, H):
    NI = D // 128
    NHC = H // 128
    NE = len(gates)
    out = np.empty((NE * NHC, 128, 2 * NI * 128), np.float32)
    for e in range(NE):
        g = np.asarray(gates[e], np.float32).reshape(NI, 128, NHC, 128).transpose(2, 1, 0, 3)
        u = np.asarray(ups[e], np.float32).reshape(NI, 128, NHC, 128).transpose(2, 1, 0, 3)
        blk = np.stack([g, u], axis=2)
        out[e * NHC:(e + 1) * NHC] = blk.reshape(NHC, 128, 2 * NI * 128)
    return out


_NC_CACHE = {}


def _get_nc():
    if "nc" not in _NC_CACHE:
        _NC_CACHE["nc"] = build_nc()
    return _NC_CACHE["nc"]


def kernel(L0_attn_x, L0_attn_Wq, L0_attn_Wk, L0_attn_Wv, L0_attn_Wo, L0_attn_bo,
           L0_ffn_shared_gate, L0_ffn_shared_up, L0_ffn_shared_down,
           L0_ffn_routed_gate, L0_ffn_routed_up, L0_ffn_routed_down,
           L0_ffn_Wr, L0_ffn_routing_bias):
    c32 = lambda a: np.ascontiguousarray(np.asarray(a), dtype=np.float32)
    x = c32(L0_attn_x)
    rg = c32(L0_ffn_routed_gate)
    ru = c32(L0_ffn_routed_up)
    wgu = pack_gateup([rg[e] for e in range(rg.shape[0])] + [c32(L0_ffn_shared_gate)],
                      [ru[e] for e in range(ru.shape[0])] + [c32(L0_ffn_shared_up)],
                      D=1024, H=rg.shape[2])
    shared = {
        "wq": c32(L0_attn_Wq), "wk": c32(L0_attn_Wk), "wv": c32(L0_attn_Wv),
        "wo": c32(L0_attn_Wo), "bo": c32(L0_attn_bo), "wr": c32(L0_ffn_Wr),
        "rb": c32(L0_ffn_routing_bias),
        "sdn": c32(L0_ffn_shared_down),
        "rdn": c32(L0_ffn_routed_down),
        "wgu": wgu,
    }
    in_maps = [dict(shared, x=np.ascontiguousarray(x[c])) for c in range(8)]
    nc = _get_nc()
    res = run_bass_kernel_spmd(nc, in_maps, core_ids=list(range(8)))
    return np.stack([res.results[c]["out"] for c in range(8)], axis=0)
